# version 01
# baseline (speedup 1.0000x reference)
import numpy as np
import ml_dtypes
import concourse.bass as bass
import concourse.mybir as mybir
from concourse.bass_utils import run_bass_kernel_spmd
from contextlib import ExitStack

F32 = mybir.dt.float32
BF16 = mybir.dt.bfloat16
U8 = mybir.dt.uint8
AF = mybir.ActivationFunctionType
ALU = mybir.AluOpType
AX = mybir.AxisListType.X
BF16NP = ml_dtypes.bfloat16

D = 2048
KCH = 16
T = 2048
HALF = 1024
NEXP = 16
NKE = 11
NKS = 22
HEADS = 16
S = 4096
RSCALE = float(1.0 / np.sqrt(128.0))
F32EPS = float(np.finfo(np.float32).eps)

DBG_SKIP = set()
TT_ORDER = [0, 1, 2, 3]
CP_LNEPS, CP_ONE, CP_HALFC, CP_NEGHALF, CP_RSC, CP_F32EPS, CP_1EM10, CP_INV2048, CP_NEG1, CP_TWO = range(10)

def tok_index(hf):
    return np.r_[0:1024, 3072:4096] if hf == 0 else np.r_[1024:3072]

def qb_glob(hf):
    return [0, 512, 3072, 3584] if hf == 0 else [1024, 1536, 2048, 2560]

class Sem:
    __slots__ = ("name", "n", "h")
    def __init__(self, name):
        self.name, self.n, self.h = name, 0, None

class Prog:
    ENG = ("sync", "gpsimd", "tensor", "vector", "scalar")

    def __init__(self, nc):
        self.nc = nc
        self.ops = {e: [] for e in self.ENG}
        self.sems = {}

    def sem(self, name):
        if name not in self.sems:
            self.sems[name] = Sem(name)
        return self.sems[name]

    def op(self, engine, fn, waits=(), inc=None, inc_amt=1):
        for s, v in waits:
            assert v <= s.n, f"[{engine}] wait {s.name}>={v} but only {s.n} scheduled"
        incs = None
        if inc is not None:
            inc.n += inc_amt
            incs = (inc, inc_amt)
        self.ops[engine].append((fn, tuple(waits), incs))
        return inc.n if inc is not None else None

    def dma(self, queue, dst, src, sem, waits=()):
        return self.op(queue, lambda e: e.dma_start(dst, src), waits=waits, inc=sem, inc_amt=16)

    def barrier(self):
        sb = self.sem("s_bar")
        for en in self.ENG:
            self.op(en, lambda e: None, inc=sb)
        tgt = sb.n
        for en in self.ENG:
            self.op(en, lambda e: None, waits=[(sb, tgt)])

    def emit(self, stack):
        for s in self.sems.values():
            s.h = stack.enter_context(self.nc.semaphore(s.name))
        with self.nc.Block() as block:
            for en in self.ENG:
                ops = self.ops[en]
                def mk(ops=ops, en=en):
                    def body(eng):
                        for fn, waits, incs in ops:
                            for s, v in waits:
                                eng.wait_ge(s.h, v)
                            inst = fn(eng)
                            if incs is not None:
                                s, amt = incs
                                if inst is None:
                                    assert en != "tensor" or amt == 1
                                    eng.sem_inc(s.h, amt)
                                else:
                                    inst.then_inc(s.h, amt)
                    return body
                getattr(block, en)(mk())

def lhsT_image(W):
    o, d_ = W.shape
    A = np.asarray(W, np.float32).reshape(o // 128, 128, d_ // 128, 128)
    return np.ascontiguousarray(A.transpose(0, 3, 2, 1))

def fm_image(x_td):
    tt, _ = np.asarray(x_td).shape
    return np.ascontiguousarray(np.asarray(x_td, np.float32).reshape(tt, KCH, 128).transpose(1, 2, 0))

def make_consts():
    c = {}
    c["id128f"] = np.eye(128, dtype=np.float32)
    c["id128b"] = np.eye(128, dtype=BF16NP)
    c["onesf"] = np.ones((128, 1), np.float32)
    c["onesb"] = np.ones((128, 1), BF16NP)
    o3 = np.zeros((65, 128), np.float32)
    o3[0, :] = 1.0; o3[32, :] = 1.0; o3[64, :] = 1.0
    c["onesrow"] = o3
    m = np.zeros((128, 4, 512), np.float32)
    kt = np.arange(128)[:, None]
    q = np.arange(512)[None, :]
    for j in range(4):
        m[:, j, :] = np.where(q >= kt + 128 * j, 0.0, -1e30)
    c["masks"] = np.ascontiguousarray(m.reshape(128, 2048))
    pool = np.zeros((1, 16), np.float32)
    pool[0, :10] = [1e-5, 1.0, 0.5, -0.5, RSCALE, F32EPS, 1e-10, 1.0 / 2048.0, -1.0, 2.0]
    c["cpool"] = pool
    dk = 128
    inv = 1.0 / (10000.0 ** (np.arange(0, dk, 2, dtype=np.float64) / dk))
    ang = np.arange(S, dtype=np.float64)[:, None] * inv[None, :]
    c["ropecos"] = np.ascontiguousarray(np.concatenate([np.cos(ang)] * 2, axis=1).T.astype(np.float32))
    c["ropesin"] = np.ascontiguousarray(np.concatenate([np.sin(ang)] * 2, axis=1).T.astype(np.float32))
    return c

CONSTS = make_consts()
CONST_SHAPES = {
    "id128f": ([128, 128], F32), "id128b": ([128, 128], BF16),
    "onesf": ([128, 1], F32), "onesb": ([128, 1], BF16), "onesrow": ([65, 128], F32),
    "masks": ([128, 2048], F32), "cpool": ([128, 16], F32),
}

class LB:
    def __init__(self, name, hf):
        self.nc = bass.Bass("TRN2", target_bir_lowering=False)
        self.P = Prog(self.nc)
        self.stack = ExitStack()
        self.hf = hf
        self.ct = {}
        self.name = name
        for s in ("s_x", "s_w", "s_xf", "s_mm", "s_ve", "s_ac", "s_st"):
            setattr(self, s, self.P.sem(s))
        self.sxp = [self.P.sem("sxp0"), self.P.sem("sxp1")]
        self.sstp = [self.P.sem("sstp0"), self.P.sem("sstp1")]
        self.s_cst = self.P.sem("s_cst")
        self.ps = [self.stack.enter_context(self.nc.psum_tensor(f"p{i}", [128, 512], F32)) for i in range(8)]
        self.inputs = {}
        self.aps = {}

    def sb(self, name, shp, dt):
        return self.stack.enter_context(self.nc.sbuf_tensor(name, shp, dt))

    def din(self, name, shp, dt, data=None):
        ap = self.nc.dram_tensor(name, shp, dt, kind="ExternalInput").ap()
        if data is not None:
            self.inputs[name] = np.ascontiguousarray(data)
        self.aps[name] = ap
        return ap

    def dout(self, name, shp, dt):
        ap = self.nc.dram_tensor(name, shp, dt, kind="ExternalOutput").ap()
        self.aps[name] = ap
        return ap

    def dint(self, name, shp, dt):
        ap = self.nc.dram_tensor(name, shp, dt, kind="Internal").ap()
        self.aps[name] = ap
        return ap

    def consts(self, names):
        for nm in names:
            shp, dt = CONST_SHAPES[nm]
            dram_shape = [1, 16] if nm == "cpool" else shp
            ap = self.din("c_" + nm, dram_shape, dt, CONSTS[nm])
            til = self.sb("sc_" + nm, shp, dt)
            src = ap.partition_broadcast(128) if nm == "cpool" else ap
            self.P.dma("sync", til[:, :], src, self.s_cst)
            self.ct[nm] = til

    def cp(self, i):
        return self.ct["cpool"][:, i:i + 1]

    def cprow(self, i):
        return self.ct["cpool"][0:1, i:i + 1]

    def cst_gate(self):
        tgt = self.s_cst.n
        for en in ("sync", "gpsimd", "tensor", "vector", "scalar"):
            self.P.op(en, lambda e: None, waits=[(self.s_cst, tgt)])

    def final_waits(self):
        for sem in self.P.sems.values():
            if sem.name.startswith("s") and sem.n > 0 and sem.name not in (
                    "s_mm", "s_ve", "s_ac", "s_xf", "s_bar"):
                self.P.op("sync", lambda e: None, waits=[(sem, sem.n)])

    def finish(self):
        self.final_waits()
        self.P.emit(self.stack)
        self.stack.close()
        return self

class Ring:
    _uid = [0]

    def __init__(self, L, slots):
        self.L = L
        self.slots = slots
        self.freed = [0] * len(slots)
        Ring._uid[0] += 1
        self.sems = [L.P.sem(f"srng{Ring._uid[0]}_{i}") for i in range(len(slots))]
        self.i = 0

    def load(self, src_ap, ncols):
        L = self.L
        j = self.i % len(self.slots)
        sl = self.slots[j]
        wts = [(L.s_mm, self.freed[j])] if self.freed[j] > 0 else []
        L.P.dma("gpsimd", sl[:, 0:ncols * 128], src_ap, self.sems[j], waits=wts)
        self.i += 1
        return j, sl, (self.sems[j], self.sems[j].n)

    def free(self, j, mmval):
        self.freed[j] = mmval

def ln_stats(L, xsrc, tw, arow, brow, xtile, x2tile, rows, acc=None):
    P = L.P
    nt = tw // 512
    assert nt in (2, 4)
    ps_s = [L.ps[i][0:1, 0:512] for i in range(nt)]
    ps_q = [L.ps[nt + i][0:1, 0:512] for i in range(nt)]
    onesf = L.ct["onesf"]
    ncons = 3 if acc is not None else 2
    base_xf = L.s_xf.n
    for k in range(KCH):
        bb = k % 2
        if k >= 2:
            w = [(L.s_xf, base_xf + ncons * (k - 1))]
        else:
            w = [(L.s_ve, L.s_ve.n), (L.s_ac, L.s_ac.n)]
        P.dma("sync", xtile[bb][:, 0:tw], xsrc(k, tw), L.sxp[bb], waits=w)
        xw = (L.sxp[bb], L.sxp[bb].n)
        P.op("scalar", (lambda bb=bb, tw=tw: lambda e: e.activation(
                x2tile[bb][:, 0:tw], xtile[bb][:, 0:tw], AF.Square))(),
             waits=[xw], inc=L.s_ac)
        acw = L.s_ac.n
        if acc is not None:
            P.op("vector", (lambda bb=bb, k=k, tw=tw: lambda e: e.tensor_scalar(
                    acc(k), xtile[bb][:, 0:tw], 2.0, None, ALU.mult))(),
                 waits=[xw], inc=L.s_xf)
        for t in range(nt):
            P.op("tensor", (lambda bb=bb, t=t, k=k: lambda e: e.matmul(
                    ps_s[t], onesf[:, 0:1], xtile[bb][:, 512 * t:512 * (t + 1)],
                    start=(k == 0), stop=(k == KCH - 1)))(),
                 waits=[xw], inc=(L.s_xf if t == nt - 1 else None))
        for t in range(nt):
            last = (k == KCH - 1 and t == nt - 1)
            P.op("tensor", (lambda bb=bb, t=t, k=k: lambda e: e.matmul(
                    ps_q[t], onesf[:, 0:1], x2tile[bb][:, 512 * t:512 * (t + 1)],
                    start=(k == 0), stop=(k == KCH - 1)))(),
                 waits=[(L.s_ac, acw)],
                 inc=(L.s_mm if last else (L.s_xf if t == nt - 1 else None)))
    mmv = L.s_mm.n
    rb = rows
    r_m = rb[0:1, 0:tw]
    r_ac = rb[0:1, 2048:2048 + tw]
    r_q = rb[32:33, 0:tw]
    r_s2 = rb[32:33, 2048:2048 + tw]
    r_a = rb[64:65, 0:tw]
    def fin(e):
        for t in range(nt):
            e.tensor_scalar(r_m[:, 512 * t:512 * (t + 1)], ps_s[t], 1.0 / 2048.0, None, ALU.mult)
        for t in range(nt - 1):
            e.tensor_scalar(r_q[:, 512 * t:512 * (t + 1)], ps_q[t], 1.0 / 2048.0, None, ALU.mult)
        inst = e.tensor_scalar(r_q[:, 512 * (nt - 1):512 * nt], ps_q[nt - 1], 1.0 / 2048.0, None, ALU.mult)
        e.drain()
        return inst
    P.op("vector", fin, waits=[(L.s_mm, mmv)], inc=L.s_ve)
    vv = L.s_ve.n
    P.op("scalar", lambda e: (e.activation(r_s2, r_m, AF.Square), e.drain())[0],
         waits=[(L.s_ve, vv)], inc=L.s_ac)
    av = L.s_ac.n
    P.op("vector", lambda e: (e.tensor_sub(r_q, r_q, r_s2), e.drain())[0],
         waits=[(L.s_ac, av)], inc=L.s_ve)
    vv = L.s_ve.n
    def fin2(e):
        e.activation(r_q, r_q, AF.Ln, bias=L.cprow(CP_LNEPS), scale=1.0)
        e.drain()
        e.activation(r_a, r_q, AF.Exp, bias=0.0, scale=-0.5)
        e.drain()
        inst = e.activation(r_ac, r_a, AF.Copy)
        e.drain()
        return inst
    P.op("scalar", fin2, waits=[(L.s_ve, vv)], inc=L.s_ac)
    av = L.s_ac.n
    def fin3(e):
        e.tensor_tensor(r_m, r_m, r_ac, ALU.mult)
        e.drain()
        inst = e.tensor_scalar(r_m, r_m, -1.0, None, ALU.mult)
        e.drain()
        return inst
    P.op("vector", fin3, waits=[(L.s_ac, av)], inc=L.s_ve)
    return L.s_ve.n

def bcast_row(L, row, dst, tw, base=0, banks=(4, 5)):
    P = L.P
    onesrow = L.ct["onesrow"]
    ac0 = L.s_ac.n
    for t in range(tw // 512):
        pb = L.ps[banks[t % 2]][:, 0:512]
        w = [(L.s_ve, L.s_ve.n), (L.s_ac, ac0)]
        P.op("tensor", (lambda t=t, pb=pb: lambda e: e.matmul(
                pb, onesrow[base:base + 1, :], row[0:1, 512 * t:512 * (t + 1)],
                start=True, stop=True))(),
             waits=w, inc=L.s_mm)
        mv = L.s_mm.n
        P.op("vector", (lambda t=t, pb=pb: lambda e: e.tensor_copy(dst[:, 512 * t:512 * (t + 1)], pb))(),
             waits=[(L.s_mm, mv)], inc=L.s_ve)
    return L.s_ve.n

def emit_moe(L, t, w, xsrc, xdst):
    P = L.P
    for hh in range(2):
        t0 = hh * HALF
        accf = lambda k: t["acc"][:, k * HALF:(k + 1) * HALF]
        rdy = ln_stats(L, lambda k, tw: xsrc[k, :, t0:t0 + tw], HALF,
                       t["arow"], t["brow"], t["xt"], t["x2t"], t["rows"], acc=accf)
        P.op("vector", lambda e: (e.tensor_scalar(t["arow2"][:, 0:HALF], t["arow"][:, 0:HALF],
                                                  0.5, None, ALU.mult), e.drain())[0],
             waits=[(L.s_ve, rdy)], inc=L.s_ve)
        bcast_row(L, t["arow2"], t["A2"], HALF, base=64)
        r3 = bcast_row(L, t["brow"], t["Bb"], HALF, base=0)
        gate_banks = [L.ps[6][0:16, 0:512], L.ps[7][0:16, 0:512]]
        for k in range(KCH):
            bb = k % 2
            wts = [(L.s_ve, r3), (L.s_mm, L.s_mm.n), (L.s_ac, L.s_ac.n)]
            def mk_t32(bb=bb, k=k):
                def f(e):
                    e.tensor_tensor(t["t32"][bb][:, 0:HALF], accf(k), t["A2"][:, 0:HALF], ALU.mult)
                    e.drain()
                    inst = e.tensor_add(t["t32"][bb][:, 0:HALF], t["t32"][bb][:, 0:HALF], t["Bb"][:, 0:HALF])
                    e.drain()
                    return inst
                return f
            P.op("vector", mk_t32(), waits=wts, inc=L.s_ve)
            tv = L.s_ve.n
            for tt in range(2):
                P.op("tensor", (lambda bb=bb, k=k, tt=tt: lambda e: e.matmul(
                        gate_banks[tt], t["gatesb"][:, k * 16:(k + 1) * 16],
                        t["t32"][bb][:, 512 * tt:512 * (tt + 1)],
                        start=(k == 0), stop=(k == KCH - 1)))(),
                     waits=[(L.s_ve, tv)], inc=(L.s_mm if tt == 1 else None))
            P.op("scalar", (lambda bb=bb, k=k: lambda e: e.activation(
                    t["xln"][:, k * HALF:(k + 1) * HALF], t["t32"][bb][:, 0:HALF], AF.Copy))(),
                 waits=[(L.s_ve, tv)], inc=L.s_ac)
        xln_rdy = L.s_ac.n
        mm_gate = L.s_mm.n
        P.op("vector", lambda e: (e.tensor_copy(t["gsb"][0:16, 0:512], gate_banks[0]),
                                  e.tensor_copy(t["gsb"][0:16, 512:1024], gate_banks[1]),
                                  e.drain())[1],
             waits=[(L.s_mm, mm_gate)], inc=L.s_ve)
        for j in range(8):
            pb = L.ps[0][:, 0:128]
            P.op("tensor", (lambda pb=pb, j=j: lambda e: e.transpose(
                    pb, t["gsb"][:, 128 * j:128 * (j + 1)], L.ct["id128f"][:, :]))(),
                 waits=[(L.s_ve, L.s_ve.n)], inc=L.s_mm)
            mv = L.s_mm.n
            def g_prep(pb=pb):
                def f(e):
                    e.tensor_copy(t["gtok"][:, :], pb[:, 0:16])
                    e.drain()
                    e.reduce_max(t["gm4"][:, 6:7], t["gtok"][:, :], axis=AX)
                    e.drain()
                    inst = e.tensor_scalar(t["gm4"][:, 7:8], t["gm4"][:, 6:7], -1.0, None, ALU.mult)
                    e.drain()
                    return inst
                return f
            P.op("vector", g_prep(), waits=[(L.s_mm, mv)], inc=L.s_ve)
            vv = L.s_ve.n
            P.op("scalar", lambda e: (e.activation(t["gp"][:, :], t["gtok"][:, :], AF.Exp,
                                                   bias=t["gm4"][:, 7:8], scale=1.0), e.drain())[0],
                 waits=[(L.s_ve, vv)], inc=L.s_ac)
            av = L.s_ac.n
            P.op("vector", lambda e: (e.tensor_copy(t["gwork"][0][:, :], t["gp"][:, :]), e.drain())[0],
                 waits=[(L.s_ac, av)], inc=L.s_ve)
            for it in range(4):
                a = t["gwork"][it % 2]
                b_ = t["gwork"][(it + 1) % 2]
                if it > 0:
                    wv = L.s_ac.n
                    P.op("vector", (lambda a=a, b_=b_: lambda e: (e.scalar_tensor_tensor(
                            a[:, :], t["gind"][:, :], -2e30, b_[:, :], ALU.mult, ALU.add), e.drain())[0])(),
                         waits=[(L.s_ac, wv)], inc=L.s_ve)
                P.op("vector", (lambda a=a, it=it: lambda e: (e.reduce_max(
                        t["gm4"][:, it:it + 1], a[:, :], axis=AX), e.drain())[0])(),
                     waits=[(L.s_ve, L.s_ve.n)], inc=L.s_ve)
                vv = L.s_ve.n
                if it < 3:
                    P.op("scalar", (lambda a=a, it=it: lambda e: (
                            e.activation(t["gsgn"][it % 2][:, :], a[:, :], AF.Sign,
                                         bias=t["gm4"][:, it:it + 1], scale=-1.0),
                            e.drain(),
                            e.activation(t["gind"][:, :], t["gsgn"][it % 2][:, :], AF.Relu,
                                         bias=L.cp(CP_ONE), scale=-1.0),
                            e.drain())[2])(),
                         waits=[(L.s_ve, vv)], inc=L.s_ac)
            P.op("scalar", lambda e: (
                    e.activation(t["gsgn"][1][:, :], t["gp"][:, :], AF.Sign, bias=t["gm4"][:, 3:4], scale=-1.0),
                    e.drain(),
                    e.activation(t["gind"][:, :], t["gsgn"][1][:, :], AF.Relu, bias=L.cp(CP_ONE), scale=-1.0),
                    e.drain())[2],
                 waits=[(L.s_ve, L.s_ve.n)], inc=L.s_ac)
            av = L.s_ac.n
            def g_cw(e):
                e.tensor_scalar(t["gind2"][:, :], t["gind"][:, :], 1.0, None, ALU.min)
                e.drain()
                e.tensor_tensor(t["cwraw"][:, :], t["gp"][:, :], t["gind2"][:, :], ALU.mult)
                e.drain()
                e.reduce_sum(t["gm4"][:, 4:5], t["cwraw"][:, :], axis=AX)
                e.drain()
                e.reciprocal(t["gm4"][:, 5:6], t["gm4"][:, 4:5])
                e.drain()
                inst = e.tensor_scalar(t["cwtok"][:, :], t["cwraw"][:, :], t["gm4"][:, 5:6], None, ALU.mult)
                e.drain()
                return inst
            P.op("vector", g_cw, waits=[(L.s_ac, av)], inc=L.s_ve)
            vv = L.s_ve.n
            pb1 = L.ps[1][0:16, 0:128]
            P.op("tensor", (lambda pb1=pb1: lambda e: e.transpose(
                    pb1, t["cwtok"][:, :], L.ct["id128f"][:, :]))(),
                 waits=[(L.s_ve, vv)], inc=L.s_mm)
            mv = L.s_mm.n
            P.op("vector", (lambda j=j, pb1=pb1: lambda e: (e.tensor_copy(
                    t["cwkm"][0:16, 128 * j:128 * (j + 1)], pb1), e.drain())[0])(),
                 waits=[(L.s_mm, mv)], inc=L.s_ve)
        cw_done = L.s_ve.n
        if "gdump" in DBG_SKIP and hh == 1:
            def gdmp(e):
                e.tensor_copy(t["gdbg"][:, 0:16], t["gtok"][:, :])
                e.tensor_copy(t["gdbg"][:, 16:32], t["gp"][:, :])
                e.tensor_copy(t["gdbg"][:, 32:40], t["gm4"][:, :])
                e.tensor_copy(t["gdbg"][:, 40:56], t["gind2"][:, :])
                inst = e.tensor_copy(t["gdbg"][:, 56:72], t["cwtok"][:, :])
                e.drain()
                return inst
            P.op("vector", gdmp, waits=[(L.s_ve, cw_done)], inc=L.s_ve)
            P.dma("sync", L.aps["gdbg"], t["gdbg"][:, :], L.s_st, waits=[(L.s_ve, L.s_ve.n)])
        s_cwd = P.sem("s_cwd")
        P.dma("sync", t["cwd"], t["cwkm"][0:16, 0:HALF], s_cwd, waits=[(L.s_ve, cw_done)])
        cwd_st = (s_cwd, s_cwd.n)

        ring = Ring(L, t["wring"])
        hblk = lambda j: t["h"][:, j * HALF:(j + 1) * HALF]

        def mlp_up(wg_ap, wu_ap, nos, bcw_wv, silu_prev, uc_prev):
            for j in range(nos):
                jg, slg, wvg = ring.load(wg_ap(j), KCH)
                ju, slu, wvu = ring.load(wu_ap(j), KCH)
                gb = [L.ps[0][:, 0:512], L.ps[1][:, 0:512]]
                ub = [L.ps[2][:, 0:512], L.ps[3][:, 0:512]]
                for tt in range(2):
                    for k in range(KCH):
                        wts = [wvg, (L.s_ac, max(xln_rdy, silu_prev))] if (tt == 0 and k == 0) else []
                        P.op("tensor", (lambda slg=slg, k=k, tt=tt, gb=gb: lambda e: e.matmul(
                                gb[tt], slg[:, 128 * k:128 * (k + 1)],
                                t["xln"][:, k * HALF + 512 * tt:k * HALF + 512 * (tt + 1)],
                                start=(k == 0), stop=(k == KCH - 1)))(),
                             waits=wts,
                             inc=(L.s_mm if (tt == 1 and k == KCH - 1) else None))
                g_mm = L.s_mm.n
                ring.free(jg, g_mm)
                for tt in range(2):
                    for k in range(KCH):
                        wts = [wvu, (L.s_ve, uc_prev)] if (tt == 0 and k == 0) else []
                        P.op("tensor", (lambda slu=slu, k=k, tt=tt, ub=ub: lambda e: e.matmul(
                                ub[tt], slu[:, 128 * k:128 * (k + 1)],
                                t["xln"][:, k * HALF + 512 * tt:k * HALF + 512 * (tt + 1)],
                                start=(k == 0), stop=(k == KCH - 1)))(),
                             waits=wts,
                             inc=(L.s_mm if (tt == 1 and k == KCH - 1) else None))
                u_mm = L.s_mm.n
                ring.free(ju, u_mm)
                P.op("scalar", (lambda gb=gb: lambda e: (
                        e.activation(t["silu"][:, 0:512], gb[0], AF.Silu),
                        e.activation(t["silu"][:, 512:1024], gb[1], AF.Silu),
                        e.drain())[1])(),
                     waits=[(L.s_mm, g_mm), (L.s_ve, uc_prev)], inc=L.s_ac)
                silu_prev = L.s_ac.n
                if bcw_wv is not None:
                    def mk_uh(ub=ub, j=j):
                        def f(e):
                            e.tensor_tensor(hblk(j)[:, 0:512], ub[0], t["bcw"][:, 0:512], ALU.mult)
                            e.tensor_tensor(hblk(j)[:, 512:1024], ub[1], t["bcw"][:, 512:1024], ALU.mult)
                            e.drain()
                            inst = e.tensor_tensor(hblk(j), t["silu"][:, 0:HALF], hblk(j), ALU.mult)
                            e.drain()
                            return inst
                        return f
                    wts = [(L.s_mm, u_mm), (L.s_ac, silu_prev), bcw_wv]
                else:
                    def mk_uh(ub=ub, j=j):
                        def f(e):
                            e.tensor_tensor(hblk(j)[:, 0:512], ub[0], t["silu"][:, 0:512], ALU.mult)
                            inst = e.tensor_tensor(hblk(j)[:, 512:1024], ub[1], t["silu"][:, 512:1024], ALU.mult)
                            e.drain()
                            return inst
                        return f
                    wts = [(L.s_mm, u_mm), (L.s_ac, silu_prev)]
                P.op("vector", mk_uh(), waits=wts, inc=L.s_ve)
                uc_prev = L.s_ve.n
            return silu_prev, uc_prev

        def mlp_down(wd_ap, nki, h_rdy, acc_prev):
            for od in range(KCH):
                jd, sld, wvd = ring.load(wd_ap(od), nki)
                db = [L.ps[6][:, 0:512], L.ps[7][:, 0:512]]
                for tt in range(2):
                    for kk in range(nki):
                        wts = [wvd, (L.s_ve, max(h_rdy, acc_prev))] if (tt == 0 and kk == 0) else []
                        P.op("tensor", (lambda sld=sld, kk=kk, tt=tt, db=db: lambda e: e.matmul(
                                db[tt], sld[:, 128 * kk:128 * (kk + 1)],
                                t["h"][:, kk * HALF + 512 * tt:kk * HALF + 512 * (tt + 1)],
                                start=(kk == 0), stop=(kk == nki - 1)))(),
                             waits=wts,
                             inc=(L.s_mm if (tt == 1 and kk == nki - 1) else None))
                mm_d = L.s_mm.n
                ring.free(jd, mm_d)
                def mk_acc(od=od, db=db):
                    def f(e):
                        e.tensor_tensor(t["acc"][:, od * HALF:od * HALF + 512],
                                        t["acc"][:, od * HALF:od * HALF + 512], db[0], ALU.add)
                        inst = e.tensor_tensor(t["acc"][:, od * HALF + 512:(od + 1) * HALF],
                                               t["acc"][:, od * HALF + 512:(od + 1) * HALF], db[1], ALU.add)
                        e.drain()
                        return inst
                    return f
                P.op("vector", mk_acc(), waits=[(L.s_mm, mm_d)], inc=L.s_ve)
                acc_prev = L.s_ve.n
            return acc_prev

        silu_prev, uc_prev, acc_prev = L.s_ac.n, L.s_ve.n, L.s_ve.n
        s_bcw = P.sem("s_bcw")
        for e_i in (range(NEXP) if "experts" not in DBG_SKIP else []):
            P.dma("gpsimd", t["bcw"][:, 0:HALF], t["cwd_e"](e_i), s_bcw,
                  waits=[cwd_st, (L.s_ve, uc_prev)])
            bcw_wv = (s_bcw, s_bcw.n)
            silu_prev, uc_prev = mlp_up(lambda j, e_i=e_i: w["wg"][e_i, j],
                                        lambda j, e_i=e_i: w["wu"][e_i, j],
                                        NKE, bcw_wv, silu_prev, uc_prev)
            acc_prev = mlp_down(lambda od, e_i=e_i: w["wd"][e_i, od], NKE, uc_prev, acc_prev)
        for part in (range(2) if "shared" not in DBG_SKIP else []):
            o0 = part * NKE
            silu_prev, uc_prev = mlp_up(lambda j, o0=o0: w["swg"][o0 + j],
                                        lambda j, o0=o0: w["swu"][o0 + j],
                                        NKE, None, silu_prev, uc_prev)
            acc_prev = mlp_down(lambda od, o0=o0: w["swd"][od, :, o0:o0 + NKE, :],
                                NKE, uc_prev, acc_prev)

        for k in range(KCH):
            bb = k % 2
            wts = [(L.s_ve, acc_prev)]
            if k >= 2:
                wts.append((L.sstp[bb], L.sstp[bb].n))
            P.op("vector", (lambda k=k, bb=bb: lambda e: e.tensor_scalar(
                    t["xo"][bb][:, 0:HALF], t["acc"][:, k * HALF:(k + 1) * HALF], 0.5, None, ALU.mult))(),
                 waits=wts, inc=L.s_ve)
            vv = L.s_ve.n
            P.dma("sync", xdst[k, :, t0:t0 + HALF], t["xo"][bb][:, 0:HALF], L.sstp[bb],
                  waits=[(L.s_ve, vv)])
        P.barrier()

def moe_bufs(L, arena1, arena2, rowbuf, A2f, Bf):
    t = {}
    f2 = arena2.bitcast(F32)
    t["acc"] = arena1.bitcast(F32)[:, 0:KCH * HALF]
    t["h"] = arena2.bitcast(BF16)[:, 8192:8192 + NKE * HALF]
    t["xln"] = L.sb("m_xln", [128, KCH * HALF], BF16)
    t["xt"] = [f2[:, 0:HALF], f2[:, HALF:2 * HALF]]
    x2 = [f2[:, 2 * HALF:3 * HALF], f2[:, 3 * HALF:4 * HALF]]
    t["x2t"] = x2
    t["t32"] = x2
    t["xo"] = [f2[:, 0:HALF], f2[:, HALF:2 * HALF]]
    t["rows"] = rowbuf
    t["arow"] = rowbuf[64:65, 0:HALF]
    t["arow2"] = rowbuf[64:65, 2048:2048 + HALF]
    t["brow"] = rowbuf[0:1, 0:HALF]
    t["A2"] = A2f[:, 0:HALF]
    t["Bb"] = Bf[:, 0:HALF]
    t["gatesb"] = L.sb("m_gatesb", [128, KCH * 16], F32)
    t["gsb"] = L.sb("m_gsb", [128, HALF], F32)
    for nm, shp in (("gtok", 16), ("gp", 16), ("gind", 16), ("gind2", 16),
                    ("cwraw", 16), ("cwtok", 16)):
        t[nm] = L.sb("m_" + nm, [128, shp], F32)
    t["gwork"] = [L.sb("m_gw0", [128, 16], F32), L.sb("m_gw1", [128, 16], F32)]
    t["gsgn"] = [L.sb("m_gs0", [128, 16], F32), L.sb("m_gs1", [128, 16], F32)]
    t["gm4"] = L.sb("m_gm4", [128, 8], F32)
    t["cwkm"] = L.sb("m_cwkm", [16, HALF], F32)
    t["bcw"] = L.sb("m_bcw", [128, HALF], F32)
    t["wring"] = [L.sb(f"m_wr{i}", [128, KCH * 128], BF16) for i in range(4)]
    t["silu"] = L.sb("m_silu", [128, HALF], BF16)
    cwd = (L.dout("cwd", [16, HALF], F32) if "cwout" in DBG_SKIP else L.dint("cwd", [16, HALF], F32))
    if "gdump" in DBG_SKIP:
        t["gdbg"] = L.sb("m_gdbg", [128, 72], F32)
        L.dout("gdbg", [128, 72], F32)
    t["cwd"] = cwd
    t["cwd_e"] = lambda e_i: cwd[e_i:e_i + 1, :].partition_broadcast(128)
    return t

def moe_weight_aps(L, inputs):
    gw = np.asarray(inputs["L0_ffn_gate_w"], np.float32)
    gimg = np.ascontiguousarray(gw.reshape(16, KCH, 128).transpose(2, 1, 0)).reshape(128, KCH * 16)
    g_ap = L.din("w_gate_img", [128, KCH * 16], F32, gimg)
    bb16 = "bf16moe" in DBG_SKIP
    dt_ = BF16 if bb16 else F32
    cv = (lambda a: a.astype(BF16NP)) if bb16 else (lambda a: a)
    wg = np.stack([lhsT_image(np.asarray(inputs["L0_ffn_w_gate"][e])) for e in range(NEXP)])
    wu = np.stack([lhsT_image(np.asarray(inputs["L0_ffn_w_up"][e])) for e in range(NEXP)])
    wd = np.stack([lhsT_image(np.asarray(inputs["L0_ffn_w_down"][e])) for e in range(NEXP)])
    aps = {"gate_img": g_ap}
    aps["wg"] = L.din("w_wg", list(wg.shape), dt_, cv(wg))
    aps["wu"] = L.din("w_wu", list(wu.shape), dt_, cv(wu))
    aps["wd"] = L.din("w_wd", list(wd.shape), dt_, cv(wd))
    aps["swg"] = L.din("w_swg", [NKS, 128, KCH, 128], dt_, cv(lhsT_image(np.asarray(inputs["L0_ffn_sw_gate"]))))
    aps["swu"] = L.din("w_swu", [NKS, 128, KCH, 128], dt_, cv(lhsT_image(np.asarray(inputs["L0_ffn_sw_up"]))))
    aps["swd"] = L.din("w_swd", [KCH, 128, NKS, 128], dt_, cv(lhsT_image(np.asarray(inputs["L0_ffn_sw_down"]))))
    return aps

def emit_proj(L, ring, wap, xln, nhout, epi, extra_first_waits=(), banks=(0, 1, 2, 3)):
    P = L.P
    for och in range(nhout):
        j, sl, wv_ = ring.load(wap[och], KCH)
        for tt in (TT_ORDER if nhout == HEADS else range(4)):
            pb = L.ps[banks[tt % 4]][:, 0:512]
            wts = [wv_, (L.s_ac, L.s_ac.n), (L.s_ve, L.s_ve.n)] + list(extra_first_waits)
            for k in range(KCH):
                P.op("tensor", (lambda sl=sl, k=k, tt=tt, pb=pb: lambda e: e.matmul(
                        pb, sl[:, 128 * k:128 * (k + 1)],
                        xln[:, k * T + 512 * tt:k * T + 512 * (tt + 1)],
                        start=(k == 0), stop=(k == KCH - 1)))(),
                     waits=wts if k == 0 else [],
                     inc=(L.s_mm if k == KCH - 1 else None))
            mmv = L.s_mm.n
            if tt == 3:
                ring.free(j, mmv)
            epi(och, tt, pb, mmv)

def emit_xln_full(L, xsrc, xlnq, xtq, t32q, A2f, Bf, arowf, browf, rowsf):
    P = L.P
    rdy = ln_stats(L, lambda k, tw: xsrc[k, :, 0:tw], T, arowf, browf, xtq, t32q, rowsf, acc=None)
    bcast_row(L, arowf, A2f, T, base=64)
    r3 = bcast_row(L, browf, Bf, T, base=0)
    for k in range(KCH):
        bb = k % 2
        P.dma("sync", xtq[bb][:, 0:T], xsrc[k, :, 0:T], L.sxp[bb],
              waits=[(L.s_ve, r3 if k < 2 else L.s_ve.n), (L.s_ac, L.s_ac.n)])
        xw = (L.sxp[bb], L.sxp[bb].n)
        def mk_x(bb=bb):
            def f(e):
                e.tensor_tensor(t32q[bb][:, 0:T], xtq[bb][:, 0:T], A2f[:, 0:T], ALU.mult)
                e.drain()
                inst = e.tensor_add(t32q[bb][:, 0:T], t32q[bb][:, 0:T], Bf[:, 0:T])
                e.drain()
                return inst
            return f
        P.op("vector", mk_x(), waits=[xw], inc=L.s_ve)
        vv = L.s_ve.n
        P.op("scalar", (lambda bb=bb, k=k: lambda e: e.activation(
                xlnq[:, k * T:(k + 1) * T], t32q[bb][:, 0:T], AF.Copy))(),
             waits=[(L.s_ve, vv)], inc=L.s_ac)
    return L.s_ac.n, r3

def build_launch_A(inputs, hf):
    L = LB("A", hf)
    L.consts(["id128f", "id128b", "onesf", "onesb", "onesrow", "cpool"])
    ar1 = L.sb("arena1", [128, 65536], U8)[:, :]
    ar2 = L.sb("arena2", [128, 39936], U8)[:, :]
    rowbuf = L.sb("rowbuf", [65, 4096], F32)
    A2f = L.sb("A2f", [128, T], F32)
    Bf = L.sb("Bf", [128, T], F32)
    t = moe_bufs(L, ar1, ar2, rowbuf, A2f, Bf)
    L.P.op("vector", lambda e: (e.memset(t["gsb"][:, :], 0.0), e.drain())[0], inc=L.s_ve)
    waps = moe_weight_aps(L, inputs)
    L.P.dma("sync", t["gatesb"][:, :], waps["gate_img"], L.s_cst)
    L.cst_gate()
    xin = L.din("xin", [KCH, 128, T], F32)
    xres_o = L.dout("xres_o", [KCH, 128, T], F32)
    wsrc = dict(wg=waps["wg"], wu=waps["wu"], wd=waps["wd"],
                swg=waps["swg"], swu=waps["swu"], swd=waps["swd"])
    emit_moe(L, t, wsrc, xin, xres_o)

    w_in = np.asarray(inputs["L0_attn_w_in"], np.float32).reshape(HEADS, 3, 128, D)
    paps = {}
    for i, nm in enumerate(("wq", "wk", "wv")):
        img = lhsT_image(np.ascontiguousarray(w_in[:, i].reshape(HEADS * 128, D)))
        paps[nm] = L.din("a_" + nm, [HEADS, 128, KCH, 128], F32, img)
    qout = L.dout("qout", [HEADS, 128, T], BF16)
    kout = L.dout("kout", [HEADS * 128, T], BF16)
    vout = L.dout("vout", [HEADS * 128, T], BF16)
    f2 = ar2.bitcast(F32)
    b2 = ar2.bitcast(BF16)
    xlnq = ar1.bitcast(BF16)[:, 0:KCH * T]
    xtq = [f2[:, 0:T], f2[:, T:2 * T]]
    t32q = [f2[:, 2 * T:3 * T], f2[:, 3 * T:4 * T]]
    vf = b2[:, 8 * T:8 * T + 512]
    stgv = b2[:, 8 * T + 512:8 * T + 1024]
    arowf = rowbuf[64:65, 0:T]
    browf = rowbuf[0:1, 0:T]
    rowsf = rowbuf
    stg = [L.sb("q_stg0", [128, 512], BF16), L.sb("q_stg1", [128, 512], BF16)]
    xln_rdy, _ = emit_xln_full(L, xres_o, xlnq, xtq, t32q, A2f, Bf, arowf, browf, rowsf)
    ring = Ring(L, t["wring"])
    P = L.P

    def epi_q(och, tt, pb, mmv):
        sg = stg[tt % 2]
        sp = L.sstp[tt % 2]
        P.op("scalar", (lambda pb=pb, sg=sg: lambda e: e.activation(
                sg[:, :], pb, AF.Copy, bias=0.0, scale=L.cp(CP_RSC)))(),
             waits=[(L.s_mm, mmv), (sp, sp.n)], inc=L.s_ac)
        av = L.s_ac.n
        P.dma("sync", qout[och, :, 512 * tt:512 * (tt + 1)], sg[:, :], sp,
              waits=[(L.s_ac, av)])

    def epi_k(och, tt, pb, mmv):
        sg = stg[tt % 2]
        sp = L.sstp[tt % 2]
        P.op("scalar", (lambda pb=pb, sg=sg: lambda e: e.activation(sg[:, :], pb, AF.Copy))(),
             waits=[(L.s_mm, mmv), (sp, sp.n)], inc=L.s_ac)
        av = L.s_ac.n
        P.dma("sync", kout[och * 128:(och + 1) * 128, 512 * tt:512 * (tt + 1)], sg[:, :],
              sp, waits=[(L.s_ac, av)])

    s_vst = P.sem("s_vst")

    def epi_v(och, tt, pb, mmv):
        P.op("scalar", (lambda pb=pb: lambda e: e.activation(vf[:, 0:512], pb, AF.Copy))(),
             waits=[(L.s_mm, mmv), (s_vst, s_vst.n)], inc=L.s_ac)
        av = L.s_ac.n
        for u in range(4):
            pbt = L.ps[6 + (u % 2)][:, :].bitcast(BF16)[:, 0:128]
            P.op("tensor", (lambda u=u, pbt=pbt: lambda e: e.transpose(
                    pbt, vf[:, 128 * u:128 * (u + 1)], L.ct["id128b"][:, :]))(),
                 waits=[(L.s_ac, av), (L.s_ve, L.s_ve.n)], inc=L.s_mm)
            mv2 = L.s_mm.n
            P.op("vector", (lambda u=u, pbt=pbt: lambda e: e.tensor_copy(
                    stgv[:, 128 * u:128 * (u + 1)], pbt))(),
                 waits=[(L.s_mm, mv2), (s_vst, s_vst.n)], inc=L.s_ve)
        vv = L.s_ve.n
        P.dma("sync", vout[och * 128:(och + 1) * 128, 512 * tt:512 * (tt + 1)],
              stgv[:, 0:512], s_vst, waits=[(L.s_ve, vv)])

    emit_proj(L, ring, paps["wq"], xlnq, HEADS, epi_q, [(L.s_ac, xln_rdy)])
    emit_proj(L, ring, paps["wk"], xlnq, HEADS, epi_k, [(L.s_ac, xln_rdy)])
    emit_proj(L, ring, paps["wv"], xlnq, HEADS, epi_v, [(L.s_ac, xln_rdy)])
    P.barrier()
    return L.finish()


def emit_attn_inner(L, b_, epi_tile):
    P = L.P
    NKB = S // 128
    r_sc = b_["rows"][32:33, 2048:2056]
    ps5 = L.ps[5]
    first = True
    s_pk = P.sem("s_pk")
    for src, n_h, w_, which in ((b_["kctx"], HEADS, S, "k"), (b_["qfl2d"], HEADS, T, "q")):
        for hh_ in range(n_h):
            P.dma("sync", b_["kt"][0][:, 0:w_], src[hh_ * 128:(hh_ + 1) * 128, 0:w_], s_pk,
                  waits=[(L.s_mm, L.s_mm.n), (L.s_ac, L.s_ac.n)])
            xw = (s_pk, s_pk.n)
            P.op("scalar", (lambda w_=w_: lambda e: e.activation(
                    b_["k2"][:, 0:w_], b_["kt"][0][:, 0:w_], AF.Square))(),
                 waits=[xw], inc=L.s_ac)
            av = L.s_ac.n
            for cc in range(w_ // 512):
                P.op("tensor", (lambda cc=cc: lambda e: e.matmul(
                        ps5[0:1, 0:512], L.ct["onesb"][:, 0:1],
                        b_["k2"][:, 512 * cc:512 * (cc + 1)], start=True, stop=True))(),
                     waits=[(L.s_ac, av), (L.s_ve, L.s_ve.n)], inc=L.s_mm)
                mv = L.s_mm.n
                fi = first
                first = False
                def mx(fi=fi, which=which):
                    def f(e):
                        e.reduce_max(r_sc[0:1, 4:5], ps5[0:1, 0:512], axis=AX)
                        e.drain()
                        if fi:
                            inst = e.tensor_copy(r_sc[0:1, 0:1] if which == "k" else r_sc[0:1, 1:2],
                                                 r_sc[0:1, 4:5])
                        else:
                            dst = r_sc[0:1, 0:1] if which == "k" else r_sc[0:1, 1:2]
                            inst = e.tensor_tensor(dst, dst, r_sc[0:1, 4:5], ALU.max)
                        e.drain()
                        return inst
                    return f
                P.op("vector", mx(), waits=[(L.s_mm, mv)], inc=L.s_ve)
        first = True
    vv = L.s_ve.n
    P.op("vector", lambda e: (e.tensor_tensor(r_sc[0:1, 2:3], r_sc[0:1, 0:1], r_sc[0:1, 1:2], ALU.mult),
                              e.drain())[0],
         waits=[(L.s_ve, vv)], inc=L.s_ve)
    vv = L.s_ve.n
    P.op("scalar", lambda e: (
            e.activation(r_sc[0:1, 2:3], r_sc[0:1, 2:3], AF.Ln, bias=0.0, scale=1.0),
            e.drain(),
            e.activation(r_sc[0:1, 3:4], r_sc[0:1, 2:3], AF.Exp, bias=0.0, scale=0.5),
            e.drain(),
            e.activation(r_sc[0:1, 3:4], r_sc[0:1, 3:4], AF.Copy, bias=0.0, scale=-1.0),
            e.drain())[4],
         waits=[(L.s_ve, vv)], inc=L.s_ac)
    av = L.s_ac.n
    P.op("tensor", lambda e: e.matmul(ps5[:, 0:1], L.ct["onesrow"][32:33, :],
                                      r_sc[0:1, 3:4], start=True, stop=True),
         waits=[(L.s_ac, av)], inc=L.s_mm)
    mv = L.s_mm.n
    P.op("vector", lambda e: (e.tensor_copy(b_["Mb"][:, 0:1], ps5[:, 0:1]), e.drain())[0],
         waits=[(L.s_mm, mv)], inc=L.s_ve)
    Mb_rdy = L.s_ve.n
    pS = [L.ps[0][:, 0:512], L.ps[1][:, 0:512]]
    pO = L.ps[2][:, 0:512]
    pl = L.ps[3][0:1, 0:512]
    s_msk = P.sem("s_msk")
    s_ctx = [P.sem("s_ctx0"), P.sem("s_ctx1")]
    for qb in range(4):
        P.dma("sync", b_["mstrip"][:, :], b_["maskin"][qb], s_msk,
              waits=[(L.s_ve, L.s_ve.n)])
        msk_w = (s_msk, s_msk.n)
        for hh_ in range(HEADS):
            kb_buf = hh_ % 2
            sc_ = s_ctx[kb_buf]
            P.dma("sync", b_["kt"][kb_buf][:, 0:S], b_["kctx"][hh_ * 128:(hh_ + 1) * 128, 0:S],
                  sc_, waits=[(L.s_mm, L.s_mm.n)])
            P.dma("sync", b_["vt"][kb_buf][:, 0:S], b_["vctx"][hh_ * 128:(hh_ + 1) * 128, 0:S],
                  sc_, waits=[])
            P.dma("sync", b_["qt"][kb_buf][:, 0:512], b_["qfl"][hh_, :, 512 * qb:512 * (qb + 1)],
                  sc_, waits=[])
            ctx_w = (sc_, sc_.n)
            l_hist = []
            for kb in range(NKB):
                sb_ = pS[kb % 2]
                P.op("tensor", (lambda sb_=sb_, kb=kb, kb_buf=kb_buf: lambda e: e.matmul(
                        sb_, b_["kt"][kb_buf][:, 128 * kb:128 * (kb + 1)],
                        b_["qt"][kb_buf][:, 0:512], start=True, stop=True))(),
                     waits=[ctx_w, (L.s_ve, Mb_rdy), (L.s_ac, L.s_ac.n)],
                     inc=L.s_mm)
                mv = L.s_mm.n
                P.op("vector", (lambda sb_=sb_, kb=kb: lambda e: (e.tensor_tensor(
                        sb_, sb_, b_["mstrip"][:, 512 * kb:512 * (kb + 1)], ALU.add), e.drain())[0])(),
                     waits=[(L.s_mm, mv), msk_w], inc=L.s_ve)
                vv = L.s_ve.n
                exp_waits = [(L.s_ve, vv)]
                if kb >= 2:
                    exp_waits.append((L.s_mm, l_hist[kb - 2]))
                P.op("scalar", (lambda sb_=sb_, kb=kb: lambda e: e.activation(
                        b_["pbuf"][kb % 2][:, :], sb_, AF.Exp, bias=b_["Mb"][:, 0:1], scale=1.0))(),
                     waits=exp_waits, inc=L.s_ac)
                av = L.s_ac.n
                P.op("tensor", (lambda kb=kb, kb_buf=kb_buf: lambda e: e.matmul(
                        pO, b_["vt"][kb_buf][:, 128 * kb:128 * (kb + 1)],
                        b_["pbuf"][kb % 2][:, :], start=(kb == 0), stop=(kb == NKB - 1)))(),
                     waits=[(L.s_ac, av)], inc=None)
                P.op("tensor", (lambda kb=kb: lambda e: e.matmul(
                        pl, L.ct["onesb"][:, 0:1], b_["pbuf"][kb % 2][:, :],
                        start=(kb == 0), stop=(kb == NKB - 1)))(),
                     waits=[], inc=L.s_mm)
                l_hist.append(L.s_mm.n)
            fl_mm = L.s_mm.n
            P.op("vector", lambda e: (e.reciprocal(b_["rows"][32:33, 2056:2568][0:1, 0:512], pl),
                                      e.drain())[0],
                 waits=[(L.s_mm, fl_mm)], inc=L.s_ve)
            vv = L.s_ve.n
            P.op("tensor", lambda e: e.matmul(L.ps[4][:, 0:512], L.ct["onesrow"][32:33, :],
                                              b_["rows"][32:33, 2056:2568][0:1, 0:512],
                                              start=True, stop=True),
                 waits=[(L.s_ve, vv)], inc=L.s_mm)
            mv = L.s_mm.n
            P.op("vector", (lambda hh_=hh_, qb=qb: lambda e: (
                    e.tensor_copy(b_["rlb"][:, 0:512], L.ps[4][:, 0:512]),
                    e.drain(),
                    e.tensor_tensor(b_["attn_out"][:, hh_ * T + 512 * qb:hh_ * T + 512 * (qb + 1)],
                                    pO, b_["rlb"][:, 0:512], ALU.mult),
                    e.drain())[2])(),
                 waits=[(L.s_mm, mv)], inc=L.s_ve)
    ring = Ring(L, b_["wring"])
    emit_proj(L, ring, b_["woT"], b_["attn_out"], HEADS, epi_tile,
              [(L.s_ve, L.s_ve.n)])

def build_launch_B1(inputs, hf):
    L = LB("B1", hf)
    L.consts(["id128f", "id128b", "onesf", "onesb", "onesrow", "cpool"])
    ar1 = L.sb("arena1", [128, 65536], U8)[:, :]
    xres_i = L.din("xres_i", [KCH, 128, T], F32)
    xres_o = L.dout("xres_o", [KCH, 128, T], F32)
    kctx = L.din("kctx", [HEADS * 128, S], BF16)
    vctx = L.din("vctx", [HEADS * 128, S], BF16)
    qfl = L.din("qfl", [HEADS, 128, T], BF16)
    maskin = L.din("maskin", [4, 128, 32 * 512], BF16)
    woT = np.asarray(inputs["L0_attn_w_out"], np.float32)
    wo_img = lhsT_image(woT)
    woap = L.din("b_woT", [KCH, 128, KCH, 128], F32, wo_img)
    b_ = {}
    b_["attn_out"] = ar1.bitcast(BF16)[:, 0:HEADS * T]
    b_["kctx"], b_["vctx"], b_["qfl"], b_["maskin"] = kctx, vctx, qfl, maskin
    b_["qfl2d"] = qfl.rearrange("h p t -> (h p) t")
    b_["woT"] = woap
    b_["kt"] = [L.sb("b_kt0", [128, S], BF16), L.sb("b_kt1", [128, S], BF16)]
    b_["vt"] = [L.sb("b_vt0", [128, S], BF16), L.sb("b_vt1", [128, S], BF16)]
    b_["qt"] = [L.sb("b_qt0", [128, 512], BF16), L.sb("b_qt1", [128, 512], BF16)]
    b_["k2"] = L.sb("b_k2", [128, S], BF16)
    b_["pbuf"] = [L.sb("b_p0", [128, 512], BF16), L.sb("b_p1", [128, 512], BF16)]
    b_["mstrip"] = L.sb("b_mstrip", [128, 32 * 512], BF16)
    b_["rlb"] = L.sb("b_rlb", [128, 512], F32)
    b_["Mb"] = L.sb("b_Mb", [128, 1], F32)
    b_["rows"] = L.sb("b_rows", [65, 4096], F32)
    b_["wring"] = [L.sb(f"b_wr{i}", [128, KCH * 128], BF16) for i in range(4)]
    L.cst_gate()
    xtile = [L.sb("b_xt0", [128, 2048], F32), L.sb("b_xt1", [128, 2048], F32)]
    stg = [L.sb("b_stg0", [128, 512], F32), L.sb("b_stg1", [128, 512], F32)]
    P = L.P

    def epi0(och, tt, pb, mmv):
        sg = stg[tt % 2]
        xb = xtile[tt % 2]
        sxp = L.sxp[tt % 2]
        sp = L.sstp[tt % 2]
        P.dma("sync", xb[:, 0:512], xres_i[och, :, 512 * tt:512 * (tt + 1)], sxp,
              waits=[(L.s_ve, L.s_ve.n)])
        xw = (sxp, sxp.n)
        P.op("vector", (lambda pb=pb, sg=sg, xb=xb: lambda e: (e.tensor_tensor(
                sg[:, :], pb, xb[:, 0:512], ALU.add), e.drain())[0])(),
             waits=[(L.s_mm, mmv), xw, (sp, sp.n)], inc=L.s_ve)
        vv = L.s_ve.n
        P.dma("sync", xres_o[och, :, 512 * tt:512 * (tt + 1)], sg[:, :], sp,
              waits=[(L.s_ve, vv)])

    emit_attn_inner(L, b_, epi0)
    P.barrier()
    return L.finish()

def host_masks(hf):
    qg = qb_glob(hf)
    m = np.zeros((4, 128, 32, 512), np.float32)
    kt = np.arange(128)[:, None]
    q = np.arange(512)[None, :]
    for i, q0 in enumerate(qg):
        for kb in range(32):
            allow = (q0 + q) >= (kb * 128 + kt)
            m[i, :, kb, :] = np.where(allow, 0.0, -1e30)
    return m.reshape(4, 128, 32 * 512).astype(BF16NP)

def host_assemble_ctx(kout_pair, vout_pair, hf_owner_map=None):
    k0, k1 = kout_pair
    out = np.empty((HEADS * 128, S), dtype=BF16NP)
    for gb in range(32):
        owner, loc = kb_owner(gb)
        src = k0 if owner == 0 else k1
        out[:, gb * 128:(gb + 1) * 128] = src[:, loc:loc + 128]
    return out

def kb_owner(gb):
    if gb < 8:
        return 0, gb * 128
    if gb >= 24:
        return 0, 1024 + (gb - 24) * 128
    return 1, (gb - 8) * 128


def emit_ffn1(L, f1, xsrc, xdst):
    P = L.P
    rb = f1["rowbuf"]
    arow = rb[64:65, 0:HALF]
    brow = rb[0:1, 0:HALF]
    for hh in range(2):
        t0 = hh * HALF
        rdy = ln_stats(L, lambda k, tw, t0=t0: xsrc[k, :, t0:t0 + tw], HALF,
                       arow, brow, f1["xt"], f1["x2t32"], rb, acc=None)
        bcast_row(L, arow, f1["A2f"][:, t0:t0 + HALF], HALF, base=64)
        r3 = bcast_row(L, brow, f1["Bf"][:, t0:t0 + HALF], HALF, base=0)
        for k in range(KCH):
            bb = k % 2
            P.dma("sync", f1["xt"][bb][:, 0:HALF], xsrc[k, :, t0:t0 + HALF], L.sxp[bb],
                  waits=[(L.s_ve, r3 if k < 2 else L.s_ve.n), (L.s_ac, L.s_ac.n)])
            xw = (L.sxp[bb], L.sxp[bb].n)
            def mk_x(bb=bb, t0=t0):
                def f(e):
                    e.tensor_tensor(f1["x2t32"][bb][:, 0:HALF], f1["xt"][bb][:, 0:HALF],
                                    f1["A2f"][:, t0:t0 + HALF], ALU.mult)
                    e.drain()
                    inst = e.tensor_add(f1["x2t32"][bb][:, 0:HALF], f1["x2t32"][bb][:, 0:HALF],
                                        f1["Bf"][:, t0:t0 + HALF])
                    e.drain()
                    return inst
                return f
            P.op("vector", mk_x(), waits=[xw], inc=L.s_ve)
            vv = L.s_ve.n
            P.op("scalar", (lambda bb=bb, k=k, t0=t0: lambda e: e.activation(
                    f1["xlnf"][:, k * T + t0:k * T + t0 + HALF], f1["x2t32"][bb][:, 0:HALF], AF.Copy))(),
                 waits=[(L.s_ve, vv)], inc=L.s_ac)
    xln_rdy = L.s_ac.n
    colsum = f1["colsum"]
    tcol = f1["tcol"]
    nwh = 0
    for k in range(KCH):
        for wh in range(2):
            bb = nwh % 2
            nwh += 1
            P.dma("sync", f1["xt"][bb][:, 0:HALF], f1["w1T"][k, :, 1024 * wh:1024 * (wh + 1)],
                  L.sxp[bb], waits=[(L.s_ve, L.s_ve.n), (L.s_ac, xln_rdy)])
            xw = (L.sxp[bb], L.sxp[bb].n)
            def mk_cs(bb=bb, first=(k == 0 and wh == 0)):
                def f(e):
                    e.reduce_sum(tcol[:, 0:1], f1["xt"][bb][:, 0:HALF], axis=AX,
                                 apply_absolute_value=True)
                    e.drain()
                    if first:
                        inst = e.tensor_copy(colsum[:, 0:1], tcol[:, 0:1])
                    else:
                        inst = e.tensor_tensor(colsum[:, 0:1], colsum[:, 0:1], tcol[:, 0:1], ALU.add)
                    e.drain()
                    return inst
                return f
            P.op("vector", mk_cs(), waits=[xw], inc=L.s_ve)
    vv = L.s_ve.n
    ps5 = L.ps[5]
    P.op("tensor", lambda e: e.matmul(ps5[0:1, 0:1], colsum[:, 0:1], L.ct["onesf"][:, 0:1],
                                      start=True, stop=True),
         waits=[(L.s_ve, vv)], inc=L.s_mm)
    mv = L.s_mm.n
    c_ = rb[0:1, 2048:2064]
    def mk_scale(e):
        e.tensor_scalar(c_[0:1, 0:1], ps5[0:1, 0:1], 2.384185791015625e-07, None, ALU.mult)
        e.drain()
        e.tensor_scalar(c_[0:1, 1:2], c_[0:1, 0:1], 1e-10, None, ALU.add)
        e.drain()
        e.reciprocal(c_[0:1, 2:3], c_[0:1, 1:2])
        e.drain()
        inst = e.tensor_scalar(c_[0:1, 3:4], c_[0:1, 0:1], 0.5, None, ALU.mult)
        e.drain()
        return inst
    P.op("vector", mk_scale, waits=[(L.s_mm, mv)], inc=L.s_ve)
    vv = L.s_ve.n
    for ci, dst in ((2, f1["rdivb"]), (3, f1["s05b"])):
        P.op("tensor", (lambda ci=ci: lambda e: e.matmul(
                ps5[:, 0:1], L.ct["onesrow"][0:1, :], c_[0:1, ci:ci + 1],
                start=True, stop=True))(),
             waits=[(L.s_ve, L.s_ve.n)], inc=L.s_mm)
        mv = L.s_mm.n
        P.op("vector", (lambda dst=dst: lambda e: (e.tensor_copy(dst[:, 0:1], ps5[:, 0:1]),
                                                   e.drain())[0])(),
             waits=[(L.s_mm, mv)], inc=L.s_ve)
    bc_rdy = L.s_ve.n
    P.op("vector", lambda e: (e.tensor_scalar(f1["b05col"][:, :], f1["bcol"][:, :], 0.5,
                                              None, ALU.mult), e.drain())[0],
         waits=[(L.s_ve, bc_rdy), (L.s_cst, L.s_cst.n)], inc=L.s_ve)
    for k in range(KCH):
        for wh in range(2):
            bb = (2 * k + wh) % 2
            wslice = f1["wt"][:, k * 2048 + 1024 * wh:k * 2048 + 1024 * (wh + 1)]
            P.dma("sync", f1["xt"][bb][:, 0:HALF], f1["w1T"][k, :, 1024 * wh:1024 * (wh + 1)],
                  L.sxp[bb], waits=[(L.s_ve, L.s_ve.n)])
            xw = (L.sxp[bb], L.sxp[bb].n)
            P.op("scalar", (lambda bb=bb, wslice=wslice: lambda e: (e.activation(
                    wslice, f1["xt"][bb][:, 0:HALF], AF.Sign), e.drain())[0])(),
                 waits=[xw, (L.s_ve, bc_rdy), (L.s_ve, L.s_ve.n)], inc=L.s_ac)
            sg_av = L.s_ac.n
            def mk_ind(bb=bb):
                def f(e):
                    e.activation(f1["rabs"][:, 0:HALF], f1["xt"][bb][:, 0:HALF], AF.Abs,
                                 bias=0.0, scale=f1["rdivb"][:, 0:1])
                    e.drain()
                    e.activation(f1["rabs"][:, 0:HALF], f1["rabs"][:, 0:HALF], AF.Sign,
                                 bias=L.cp(CP_NEGHALF), scale=1.0)
                    e.drain()
                    inst = e.activation(f1["indb"][:, 0:HALF], f1["rabs"][:, 0:HALF], AF.Relu)
                    e.drain()
                    return inst
                return f
            P.op("scalar", mk_ind(), waits=[(L.s_ac, sg_av)], inc=L.s_ac)
            av = L.s_ac.n
            P.op("vector", (lambda wslice=wslice: lambda e: (e.tensor_tensor(
                    wslice, f1["indb"][:, 0:HALF], wslice, ALU.mult), e.drain())[0])(),
                 waits=[(L.s_ac, av)], inc=L.s_ve)
    wt_rdy = L.s_ve.n
    for och in range(KCH):
        for tt in range(4):
            pb = L.ps[tt % 4][:, 0:512]
            wts = [(L.s_ve, wt_rdy), (L.s_ac, L.s_ac.n), (L.s_ve, L.s_ve.n)]
            for k in range(KCH):
                P.op("tensor", (lambda k=k, tt=tt, pb=pb, och=och: lambda e: e.matmul(
                        pb, f1["wt"][:, (k * KCH + och) * 128:(k * KCH + och) * 128 + 128],
                        f1["xlnf"][:, k * T + 512 * tt:k * T + 512 * (tt + 1)],
                        start=(k == 0), stop=(k == KCH - 1)))(),
                     waits=wts if k == 0 else [],
                     inc=(L.s_mm if k == KCH - 1 else None))
            mmv = L.s_mm.n
            sg = f1["stg"][tt % 2]
            xb = f1["xt"][tt % 2]
            sxp = L.sxp[tt % 2]
            sp = L.sstp[tt % 2]
            P.dma("sync", xb[:, 0:512], xsrc[och, :, 512 * tt:512 * (tt + 1)], sxp,
                  waits=[(L.s_ve, L.s_ve.n)])
            xw = (sxp, sxp.n)
            P.op("scalar", (lambda pb=pb, och=och: lambda e: (e.activation(
                    f1["rabs"][:, 0:512], pb, AF.Identity,
                    bias=f1["b05col"][:, och:och + 1], scale=f1["s05b"][:, 0:1]), e.drain())[0])(),
                 waits=[(L.s_mm, mmv), (L.s_ve, L.s_ve.n)], inc=L.s_ac)
            av = L.s_ac.n
            P.op("vector", (lambda sg=sg, xb=xb: lambda e: (e.tensor_tensor(
                    sg[:, 0:512], xb[:, 0:512], f1["rabs"][:, 0:512],
                    ALU.add), e.drain())[0])(),
                 waits=[(L.s_ac, av), xw, (sp, sp.n)], inc=L.s_ve)
            vv = L.s_ve.n
            P.dma("sync", xdst[och, :, 512 * tt:512 * (tt + 1)], sg[:, 0:512], sp,
                  waits=[(L.s_ve, vv)])
    P.barrier()

def emit_xln_gen(L, xsrc, xlnq, xtq, t32q, A2f, Bf, rowbuf):
    P = L.P
    arowf = rowbuf[64:65, 0:T]
    browf = rowbuf[0:1, 0:T]
    rdy = ln_stats(L, lambda k, tw: xsrc[k, :, 0:tw], T, arowf, browf, xtq, t32q, rowbuf, acc=None)
    bcast_row(L, arowf, A2f, T, base=64)
    r3 = bcast_row(L, browf, Bf, T, base=0)
    for k in range(KCH):
        bb = k % 2
        P.dma("sync", xtq[bb][:, 0:T], xsrc[k, :, 0:T], L.sxp[bb],
              waits=[(L.s_ve, r3 if k < 2 else L.s_ve.n), (L.s_ac, L.s_ac.n)])
        xw = (L.sxp[bb], L.sxp[bb].n)
        def mk_x(bb=bb):
            def f(e):
                e.tensor_tensor(t32q[bb][:, 0:T], xtq[bb][:, 0:T], A2f[:, 0:T], ALU.mult)
                e.drain()
                inst = e.tensor_add(t32q[bb][:, 0:T], t32q[bb][:, 0:T], Bf[:, 0:T])
                e.drain()
                return inst
            return f
        P.op("vector", mk_x(), waits=[xw], inc=L.s_ve)
        vv = L.s_ve.n
        P.op("scalar", (lambda bb=bb, k=k: lambda e: e.activation(
                xlnq[:, k * T:(k + 1) * T], t32q[bb][:, 0:T], AF.Copy))(),
             waits=[(L.s_ve, vv)], inc=L.s_ac)
    return L.s_ac.n, r3

def emit_attn1pre(L, a1, xsrc):
    P = L.P
    rb = a1["rowbuf"]
    arow = rb[64:65, 0:HALF]
    brow = rb[0:1, 0:HALF]
    r_q = rb[32:33, 0:T]
    s_ab = P.sem("s_ab")
    for hh in range(2):
        t0 = hh * HALF
        if hh == 1:
            P.op("vector", lambda e: e.engine_nop(), waits=[(s_ab, s_ab.n)], inc=L.s_ve)
        rdy = ln_stats(L, lambda k, tw, t0=t0: xsrc[k, :, t0:t0 + tw], HALF,
                       arow, brow, a1["xt"], a1["x2t32"], rb, acc=None)
        P.dma("sync", a1["abrows"][0:1, t0:t0 + HALF], rb[0:1, 0:HALF], s_ab,
              waits=[(L.s_ve, rdy)])
        P.dma("sync", a1["abrows"][1:2, t0:t0 + HALF], rb[64:65, 0:HALF], s_ab,
              waits=[(L.s_ve, rdy)])
        bcast_row(L, arow, a1["A2f"][:, t0:t0 + HALF], HALF, base=64)
        r3 = bcast_row(L, brow, a1["Bf"][:, t0:t0 + HALF], HALF, base=0)
        ps_sq = [L.ps[0][0:1, 0:512], L.ps[1][0:1, 0:512]]
        for k in range(KCH):
            bb = k % 2
            P.dma("sync", a1["xt"][bb][:, 0:HALF], xsrc[k, :, t0:t0 + HALF], L.sxp[bb],
                  waits=[(L.s_ve, r3 if k < 2 else L.s_ve.n), (L.s_mm, L.s_mm.n),
                         (L.s_ac, L.s_ac.n)])
            xw = (L.sxp[bb], L.sxp[bb].n)
            def mk_x(bb=bb, t0=t0):
                def f(e):
                    e.tensor_tensor(a1["x2t32"][bb][:, 0:HALF], a1["xt"][bb][:, 0:HALF],
                                    a1["A2f"][:, t0:t0 + HALF], ALU.mult)
                    e.drain()
                    inst = e.tensor_add(a1["x2t32"][bb][:, 0:HALF], a1["x2t32"][bb][:, 0:HALF],
                                        a1["Bf"][:, t0:t0 + HALF])
                    e.drain()
                    return inst
                return f
            P.op("vector", mk_x(), waits=[xw], inc=L.s_ve)
            vv = L.s_ve.n
            P.op("scalar", (lambda bb=bb: lambda e: (e.activation(
                    a1["sqb"][:, 0:HALF], a1["x2t32"][bb][:, 0:HALF], AF.Square), e.drain())[0])(),
                 waits=[(L.s_ve, vv)], inc=L.s_ac)
            av = L.s_ac.n
            for t in range(2):
                P.op("tensor", (lambda t=t, k=k: lambda e: e.matmul(
                        ps_sq[t], L.ct["onesf"][:, 0:1], a1["sqb"][:, 512 * t:512 * (t + 1)],
                        start=(k == 0), stop=(k == KCH - 1)))(),
                     waits=[(L.s_ac, av)],
                     inc=(L.s_mm if (k == KCH - 1 and t == 1) else None))
        mmv = L.s_mm.n
        def mk_ms(t0=t0):
            def f(e):
                e.tensor_scalar(r_q[0:1, t0:t0 + 512], ps_sq[0], 1.0 / 2048.0, None, ALU.mult)
                inst = e.tensor_scalar(r_q[0:1, t0 + 512:t0 + 1024], ps_sq[1], 1.0 / 2048.0,
                                       None, ALU.mult)
                e.drain()
                return inst
            return f
        P.op("vector", mk_ms(), waits=[(L.s_mm, mmv)], inc=L.s_ve)
        vv = L.s_ve.n
        def mk_rstd2(t0=t0):
            def f(e):
                e.activation(r_q[0:1, t0:t0 + HALF], r_q[0:1, t0:t0 + HALF], AF.Ln,
                             bias=L.cprow(CP_F32EPS), scale=1.0)
                e.drain()
                inst = e.activation(r_q[0:1, t0:t0 + HALF], r_q[0:1, t0:t0 + HALF], AF.Exp,
                                    bias=0.0, scale=-0.5)
                e.drain()
                return inst
            return f
        P.op("scalar", mk_rstd2(), waits=[(L.s_ve, vv)], inc=L.s_ac)
        av = L.s_ac.n
        P.op("vector", lambda e: e.engine_nop(), waits=[(L.s_ac, av)], inc=L.s_ve)
        bcast_row(L, r_q[0:1, t0:t0 + HALF], a1["R2b"][:, t0:t0 + HALF], HALF, base=32)
        r2b_rdy = L.s_ve.n
        for k in range(KCH):
            bb = k % 2
            P.dma("sync", a1["xt"][bb][:, 0:HALF], xsrc[k, :, t0:t0 + HALF], L.sxp[bb],
                  waits=[(L.s_ve, L.s_ve.n), (L.s_ac, L.s_ac.n)])
            xw = (L.sxp[bb], L.sxp[bb].n)
            def mk_xn(bb=bb, k=k, t0=t0):
                def f(e):
                    e.tensor_tensor(a1["x2t32"][bb][:, 0:HALF], a1["xt"][bb][:, 0:HALF],
                                    a1["A2f"][:, t0:t0 + HALF], ALU.mult)
                    e.drain()
                    e.tensor_add(a1["x2t32"][bb][:, 0:HALF], a1["x2t32"][bb][:, 0:HALF],
                                 a1["Bf"][:, t0:t0 + HALF])
                    e.drain()
                    e.tensor_tensor(a1["x2t32"][bb][:, 0:HALF], a1["x2t32"][bb][:, 0:HALF],
                                    a1["R2b"][:, t0:t0 + HALF], ALU.mult)
                    e.drain()
                    inst = e.tensor_scalar(a1["Xn"][:, k * T + t0:k * T + t0 + HALF],
                                           a1["x2t32"][bb][:, 0:HALF],
                                           a1["rmscol"][:, k:k + 1], None, ALU.mult)
                    e.drain()
                    return inst
                return f
            P.op("vector", mk_xn(), waits=[xw, (L.s_ve, r2b_rdy), (L.s_cst, L.s_cst.n)], inc=L.s_ve)
    xn_rdy = L.s_ve.n
    ring = Ring(L, a1["wring"])
    qf32 = a1["qf32"]
    rot, tr1, tr2 = a1["rot"], a1["tr1"], a1["tr2"]
    def epi(och, tt, pb, mmv):
        P.op("scalar", (lambda pb=pb, och=och, tt=tt: lambda e: (e.activation(
                qf32[:, 512 * tt:512 * (tt + 1)], pb, AF.Identity,
                bias=a1["wqbcol"][:, och:och + 1], scale=1.0), e.drain())[0])(),
             waits=[(L.s_mm, mmv), (L.s_ve, L.s_ve.n), (L.s_cst, L.s_cst.n)], inc=L.s_ac)

        if tt < 3:
            return
        av = L.s_ac.n
        if "a1dbg" in DBG_SKIP and och == 15:
            P.dma("sync", a1["qfdbg"], qf32[:, 0:T], L.s_st, waits=[(L.s_ac, av)])
        def mk_rot(e):
            e.activation(rot[0:64, 0:T], qf32[64:128, 0:T], AF.Copy, bias=0.0, scale=-1.0)
            e.drain()
            inst = e.activation(rot[64:128, 0:T], qf32[0:64, 0:T], AF.Copy)
            e.drain()
            return inst
        P.op("scalar", mk_rot, waits=[(L.s_ac, av)], inc=L.s_ac)
        av = L.s_ac.n
        def mk_qr(e):
            e.tensor_tensor(tr1[:, 0:T], qf32[:, 0:T], a1["cosT"][:, 0:T], ALU.mult)
            e.tensor_tensor(tr2[:, 0:T], rot[:, 0:T], a1["sinT"][:, 0:T], ALU.mult)
            e.drain()
            inst = e.tensor_add(tr1[:, 0:T], tr1[:, 0:T], tr2[:, 0:T])
            e.drain()
            return inst
        skq = P.sem("s_kq1")
        svv = P.sem("s_vv1")
        P.op("vector", mk_qr, waits=[(L.s_ac, av), (skq, skq.n), (L.s_cst, L.s_cst.n)], inc=L.s_ve)
        vv = L.s_ve.n
        def mk_out(e):
            e.activation(a1["stgk"][:, 0:T], tr1[:, 0:T], AF.Copy)
            e.drain()
            inst = e.activation(a1["stgq"][:, 0:T], tr1[:, 0:T], AF.Copy, bias=0.0,
                                scale=L.cp(CP_RSC))
            e.drain()
            return inst
        P.op("scalar", mk_out, waits=[(L.s_ve, vv), (skq, skq.n)], inc=L.s_ac)
        av = L.s_ac.n
        P.dma("sync", a1["k1out"][och * 128:(och + 1) * 128, 0:T], a1["stgk"][:, 0:T], skq,
              waits=[(L.s_ac, av)])
        P.dma("sync", a1["q1out"][och, :, 0:T], a1["stgq"][:, 0:T], skq,
              waits=[(L.s_ac, av)])
        P.op("scalar", lambda e: (e.activation(a1["vb"][:, 0:T], qf32[:, 0:T], AF.Copy),
                                  e.drain())[0],
             waits=[(L.s_ve, L.s_ve.n), (svv, svv.n)], inc=L.s_ac)
        av = L.s_ac.n
        for u in range(KCH):
            pbt = L.ps[6 + (u % 2)][:, :].bitcast(BF16)[:, 0:128]
            P.op("tensor", (lambda u=u, pbt=pbt: lambda e: e.transpose(
                    pbt, a1["vb"][:, 128 * u:128 * (u + 1)], L.ct["id128b"][:, :]))(),
                 waits=[(L.s_ac, av), (L.s_ve, L.s_ve.n)], inc=L.s_mm)
            mv2 = L.s_mm.n
            P.op("vector", (lambda u=u, pbt=pbt: lambda e: e.tensor_copy(
                    a1["stgv"][:, 128 * u:128 * (u + 1)], pbt))(),
                 waits=[(L.s_mm, mv2), (svv, svv.n)], inc=L.s_ve)
        vv = L.s_ve.n
        P.dma("sync", a1["v1out"][och * 128:(och + 1) * 128, 0:T], a1["stgv"][:, 0:T], svv,
              waits=[(L.s_ve, vv)])
    if "a1dbg" in DBG_SKIP:
        s_xnd = P.sem("s_xnd")
        P.dma("sync", a1["xnfull"], a1["Xn"][:, 0:KCH * T], s_xnd, waits=[(L.s_ve, xn_rdy)])
    emit_proj(L, ring, a1["wq1T"], a1["Xn"], HEADS, epi, [(L.s_ve, xn_rdy)], banks=(2, 3, 4, 5))
    P.barrier()

def build_launch_B2(inputs, hf):
    L = LB("B2", hf)
    L.consts(["id128f", "id128b", "onesf", "onesb", "onesrow", "cpool"])
    ar1 = L.sb("arena1", [128, 65536], U8)[:, :]
    ar2 = L.sb("arena2", [128, 71680], U8)[:, :]
    rowbuf = L.sb("rowbuf", [65, 4096], F32)
    xts = [L.sb(f"s_xt{i}", [128, HALF], F32) for i in range(2)]
    x2s = [L.sb(f"s_x2{i}", [128, HALF], F32) for i in range(2)]
    t = moe_bufs2(L, ar1, ar2, rowbuf, xts, x2s)
    L.P.op("vector", lambda e: (e.memset(t["gsb"][:, :], 0.0), e.drain())[0], inc=L.s_ve)
    waps = moe_weight_aps(L, inputs)
    L.P.dma("sync", t["gatesb"][:, :], waps["gate_img"], L.s_cst)
    L.cst_gate()
    xres_i = L.din("xres_i", [KCH, 128, T], F32)
    xint = L.dint("xint", [KCH, 128, T], F32)
    xres_o = L.dout("xres_o", [KCH, 128, T], F32)
    wsrc = dict(wg=waps["wg"], wu=waps["wu"], wd=waps["wd"],
                swg=waps["swg"], swu=waps["swu"], swd=waps["swd"])
    emit_moe(L, t, wsrc, xres_i, xint)
    w1img = np.ascontiguousarray(np.asarray(inputs["L1_ffn_weight"], np.float32).T.reshape(KCH, 128, D))
    f1 = {}
    f1["w1T"] = L.din("w1T", [KCH, 128, D], F32, w1img)
    bcolimg = np.ascontiguousarray(np.asarray(inputs["L1_ffn_bias"], np.float32).reshape(KCH, 128).T)
    f1["bcol"] = L.sb("f_bcol", [128, KCH], F32)
    L.P.dma("sync", f1["bcol"][:, :], L.din("w1bcol", [128, KCH], F32, bcolimg), L.s_cst)
    f1["wt"] = ar1.bitcast(BF16)[:, 0:KCH * 2048]
    f1["xlnf"] = ar2.bitcast(BF16)[:, 0:KCH * T]
    f1["xt"] = xts
    f1["x2t32"] = x2s
    f1["rabs"] = L.sb("f_rabs", [128, HALF], F32)
    f1["indb"] = L.sb("f_indb", [128, HALF], BF16)
    f1["colsum"] = L.sb("f_colsum", [128, 1], F32)
    f1["tcol"] = L.sb("f_tcol", [128, 1], F32)
    f1["rdivb"] = L.sb("f_rdivb", [128, 1], F32)
    f1["s05b"] = L.sb("f_s05b", [128, 1], F32)
    f1["b05col"] = L.sb("f_b05col", [128, KCH], F32)
    f2v_ = ar2.bitcast(F32)
    f1["stg"] = [f2v_[:, 16384:16896], f2v_[:, 16896:17408]]
    f1["rowbuf"] = rowbuf
    ar1f = ar1.bitcast(F32)
    f1["A2f"], f1["Bf"] = ar1f[:, 4096:6144], ar1f[:, 6144:8192]
    emit_ffn1(L, f1, xint, xres_o)
    a1 = {}
    a1["rowbuf"] = rowbuf
    f2v = ar2.bitcast(F32)
    a1["A2f"], a1["Bf"] = f2v[:, T:2 * T], f2v[:, 2 * T:3 * T]
    a1["R2b"] = f2v[:, 3 * T:4 * T]
    a1["xt"] = xts
    a1["x2t32"] = x2s
    a1["sqb"] = f1["rabs"]
    a1["Xn"] = ar1.bitcast(BF16)[:, 0:KCH * T]
    b2v = ar2.bitcast(BF16)
    a1["qf32"] = f2v[:, 0:T]
    a1["rot"] = f2v[:, T:2 * T]
    a1["tr1"] = f2v[:, 2 * T:3 * T]
    a1["tr2"] = f2v[:, 3 * T:4 * T]
    a1["cosT"] = f2v[:, 4 * T:5 * T]
    a1["sinT"] = f2v[:, 5 * T:6 * T]
    a1["vb"] = b2v[:, 12 * T:13 * T]
    a1["stgv"] = b2v[:, 13 * T:14 * T]
    a1["stgk"] = b2v[:, 14 * T:15 * T]
    a1["stgq"] = b2v[:, 15 * T:16 * T]
    rmsimg = np.ascontiguousarray(np.asarray(inputs["L1_attn_rms_w"], np.float32).reshape(KCH, 128).T)
    a1["rmscol"] = L.sb("a_rms", [128, KCH], F32)
    L.P.dma("sync", a1["rmscol"][:, :], L.din("a_rmsimg", [128, KCH], F32, rmsimg), L.s_cst)
    wqbimg = np.ascontiguousarray(np.asarray(inputs["L1_attn_Wq_b"], np.float32).reshape(KCH, 128).T)
    a1["wqbcol"] = L.sb("a_wqb", [128, KCH], F32)
    L.P.dma("sync", a1["wqbcol"][:, :], L.din("a_wqbimg", [128, KCH], F32, wqbimg), L.s_cst)
    if "bf16wq" in DBG_SKIP:
        img = lhsT_image(np.asarray(inputs["L1_attn_Wq_w"], np.float32)).astype(BF16NP)
        a1["wq1T"] = L.din("a_wq1T", [HEADS, 128, KCH, 128], BF16, img)
    else:
        a1["wq1T"] = L.din("a_wq1T", [HEADS, 128, KCH, 128], F32,
                           lhsT_image(np.asarray(inputs["L1_attn_Wq_w"], np.float32)))
    a1["wring"] = t["wring"]
    a1["abrows"] = L.dout("abrows", [2, T], F32)
    if "a1dbg" in DBG_SKIP:
        a1["xnfull"] = L.dout("xnfull", [128, KCH * T], BF16)

        a1["qfdbg"] = L.dout("qfdbg", [128, T], F32)
    a1["k1out"] = L.dout("k1out", [HEADS * 128, T], BF16)
    a1["q1out"] = L.dout("q1out", [HEADS, 128, T], BF16)
    a1["v1out"] = L.dout("v1out", [HEADS * 128, T], BF16)
    a1["cos_in"] = L.din("ropecos_own", [128, T], F32)
    a1["sin_in"] = L.din("ropesin_own", [128, T], F32)
    L.P.dma("sync", a1["cosT"][:, 0:T], a1["cos_in"], L.s_cst)
    L.P.dma("sync", a1["sinT"][:, 0:T], a1["sin_in"], L.s_cst)
    emit_attn1pre(L, a1, xres_o)
    return L.finish()

def moe_bufs2(L, arena1, arena2, rowbuf, xts, x2s):
    t = {}
    f2 = arena2.bitcast(F32)
    b2 = arena2.bitcast(BF16)
    t["acc"] = arena1.bitcast(F32)[:, 0:KCH * HALF]
    t["h"] = b2[:, 8192:8192 + NKE * HALF]
    t["xln"] = b2[:, 19456:19456 + KCH * HALF]
    t["xt"] = xts
    t["x2t"] = x2s
    t["t32"] = x2s
    t["xo"] = xts
    t["rows"] = rowbuf
    t["arow"] = rowbuf[64:65, 0:HALF]
    t["arow2"] = rowbuf[64:65, 2048:2048 + HALF]
    t["brow"] = rowbuf[0:1, 0:HALF]
    t["A2"] = f2[:, 0:HALF]
    t["Bb"] = f2[:, HALF:2 * HALF]
    t["gatesb"] = L.sb("m_gatesb", [128, KCH * 16], F32)
    t["gsb"] = L.sb("m_gsb", [128, HALF], F32)
    for nm, shp in (("gtok", 16), ("gp", 16), ("gind", 16), ("gind2", 16),
                    ("cwraw", 16), ("cwtok", 16)):
        t[nm] = L.sb("m_" + nm, [128, shp], F32)
    t["gwork"] = [L.sb("m_gw0", [128, 16], F32), L.sb("m_gw1", [128, 16], F32)]
    t["gsgn"] = [L.sb("m_gs0", [128, 16], F32), L.sb("m_gs1", [128, 16], F32)]
    t["gm4"] = L.sb("m_gm4", [128, 8], F32)
    t["cwkm"] = L.sb("m_cwkm", [16, HALF], F32)
    t["bcw"] = L.sb("m_bcw", [128, HALF], F32)
    t["wring"] = [L.sb(f"m_wr{i}", [128, KCH * 128], BF16) for i in range(4)]
    t["silu"] = L.sb("m_silu", [128, HALF], BF16)
    cwd = L.dint("cwd", [16, HALF], F32)
    t["cwd"] = cwd
    t["cwd_e"] = lambda e_i: cwd[e_i:e_i + 1, :].partition_broadcast(128)
    return t

def build_launch_C(inputs, hf):
    L = LB("C", hf)
    L.consts(["id128f", "id128b", "onesf", "onesb", "onesrow", "cpool"])
    ar1 = L.sb("arena1", [128, 65536], U8)[:, :]
    ar2 = L.sb("arena2", [128, 75776], U8)[:, :]
    rowbuf = L.sb("rowbuf", [65, 4096], F32)
    A2f = L.sb("A2f", [128, T], F32)
    Bf = L.sb("Bf", [128, T], F32)
    xres_i = L.din("xres_i", [KCH, 128, T], F32)
    xint = L.dint("xint", [KCH, 128, T], F32)
    xout = L.dout("xout", [KCH, 128, T], F32)
    kctx = L.din("kctx", [HEADS * 128, S], BF16)
    vctx = L.din("vctx", [HEADS * 128, S], BF16)
    qfl = L.din("qfl", [HEADS, 128, T], BF16)
    maskin = L.din("maskin", [4, 128, 32 * 512], BF16)
    abrows = L.din("abrows", [2, T], F32)
    wo_img = lhsT_image(np.asarray(inputs["L1_attn_Wo_w"], np.float32))
    woap = L.din("c_woT", [KCH, 128, KCH, 128], F32, wo_img)
    wobimg = np.ascontiguousarray(np.asarray(inputs["L1_attn_Wo_b"], np.float32).reshape(KCH, 128).T)
    wobcol = L.sb("c_wob", [128, KCH], F32)
    L.P.dma("sync", wobcol[:, :], L.din("c_wobimg", [128, KCH], F32, wobimg), L.s_cst)
    L.P.dma("sync", rowbuf[0:1, 0:T], abrows[0:1, :], L.s_cst)
    L.P.dma("sync", rowbuf[64:65, 0:T], abrows[1:2, :], L.s_cst)
    ab_w = L.s_cst.n
    L.P.op("vector", lambda e: e.engine_nop(), waits=[(L.s_cst, ab_w)], inc=L.s_ve)
    bcast_row(L, rowbuf[64:65, 0:T], A2f, T, base=64)
    r3 = bcast_row(L, rowbuf[0:1, 0:T], Bf, T, base=0)
    L.P.op("vector", lambda e: (e.tensor_scalar(A2f[:, 0:T], A2f[:, 0:T], 1.0, None, ALU.add),
                                e.drain())[0],
           waits=[(L.s_ve, r3)], inc=L.s_ve)
    a1f_rdy = L.s_ve.n
    L.cst_gate()

    b_ = {}
    b_["attn_out"] = ar1.bitcast(BF16)[:, 0:HEADS * T]
    b_["kctx"], b_["vctx"], b_["qfl"], b_["maskin"] = kctx, vctx, qfl, maskin
    b_["qfl2d"] = qfl.rearrange("h p t -> (h p) t")
    b_["woT"] = woap
    b2v = ar2.bitcast(BF16)
    b_["kt"] = [b2v[:, 0:S], b2v[:, S:2 * S]]
    b_["vt"] = [b2v[:, 2 * S:3 * S], b2v[:, 3 * S:4 * S]]
    b_["k2"] = b2v[:, 4 * S:5 * S]
    b_["mstrip"] = b2v[:, 5 * S:5 * S + 32 * 512]
    b_["qt"] = [L.sb("c_qt0", [128, 512], BF16), L.sb("c_qt1", [128, 512], BF16)]
    b_["pbuf"] = [L.sb("c_p0", [128, 512], BF16), L.sb("c_p1", [128, 512], BF16)]
    b_["rlb"] = L.sb("c_rlb", [128, 512], F32)
    b_["Mb"] = L.sb("c_Mb", [128, 1], F32)
    b_["rows"] = rowbuf
    b_["wring"] = [L.sb(f"c_wr{i}", [128, KCH * 128], BF16) for i in range(4)]
    xtile = [L.sb("c_xt0", [128, HALF], F32), L.sb("c_xt1", [128, HALF], F32)]
    ar1f = ar1.bitcast(F32)
    x2t32 = [ar1f[:, 4096:5120], ar1f[:, 5120:6144]]
    stg = [L.sb("c_stg0", [128, 512], F32), L.sb("c_stg1", [128, 512], F32)]
    P = L.P

    def epi1(och, tt, pb, mmv):
        sg = stg[tt % 2]
        xb = xtile[tt % 2]
        sxp = L.sxp[tt % 2]
        sp = L.sstp[tt % 2]
        P.dma("sync", xb[:, 0:512], xres_i[och, :, 512 * tt:512 * (tt + 1)], sxp,
              waits=[(L.s_ve, L.s_ve.n)])
        xw = (sxp, sxp.n)
        P.op("scalar", (lambda pb=pb, och=och, tt=tt: lambda e: (e.activation(
                b_["rlb"][:, 0:512], pb, AF.Identity, bias=wobcol[:, och:och + 1], scale=1.0),
                e.drain())[0])(),
             waits=[(L.s_mm, mmv), (L.s_ve, L.s_ve.n), (L.s_cst, L.s_cst.n)], inc=L.s_ac)
        av = L.s_ac.n
        def mk_res(sg=sg, xb=xb, tt=tt):
            def f(e):
                e.tensor_tensor(sg[:, 0:512], xb[:, 0:512], A2f[:, 512 * tt:512 * (tt + 1)], ALU.mult)
                e.drain()
                e.tensor_add(sg[:, 0:512], sg[:, 0:512], Bf[:, 512 * tt:512 * (tt + 1)])
                e.drain()
                inst = e.tensor_add(sg[:, 0:512], sg[:, 0:512], b_["rlb"][:, 0:512])
                e.drain()
                return inst
            return f
        P.op("vector", mk_res(), waits=[(L.s_ac, av), xw, (sp, sp.n),
                                        (L.s_ve, a1f_rdy)], inc=L.s_ve)
        vv = L.s_ve.n
        P.dma("sync", xint[och, :, 512 * tt:512 * (tt + 1)], sg[:, 0:512], sp,
              waits=[(L.s_ve, vv)])

    emit_attn_inner(L, b_, epi1)
    P.barrier()
    w1img = np.ascontiguousarray(np.asarray(inputs["L1_ffn_weight"], np.float32).T.reshape(KCH, 128, D))
    f1 = {}
    f1["w1T"] = L.din("w1T", [KCH, 128, D], F32, w1img)
    bcolimg = np.ascontiguousarray(np.asarray(inputs["L1_ffn_bias"], np.float32).reshape(KCH, 128).T)
    f1["bcol"] = L.sb("f_bcol", [128, KCH], F32)
    L.P.dma("sync", f1["bcol"][:, :], L.din("w1bcol", [128, KCH], F32, bcolimg), L.s_cst)
    f1["wt"] = ar1.bitcast(BF16)[:, 0:KCH * 2048]
    f1["xlnf"] = ar2.bitcast(BF16)[:, 0:KCH * T]
    f1["xt"] = xtile
    f1["x2t32"] = x2t32
    ar2f = ar2.bitcast(F32)
    ar2b = ar2.bitcast(BF16)
    f1["rabs"] = ar2f[:, 16384:17408]
    f1["indb"] = ar2b[:, 34816:35840]
    f1["colsum"] = L.sb("f_colsum", [128, 1], F32)
    f1["tcol"] = L.sb("f_tcol", [128, 1], F32)
    f1["rdivb"] = L.sb("f_rdivb", [128, 1], F32)
    f1["s05b"] = L.sb("f_s05b", [128, 1], F32)
    f1["b05col"] = L.sb("f_b05col", [128, KCH], F32)
    f1["stg"] = stg
    f1["rowbuf"] = rowbuf
    f1["A2f"], f1["Bf"] = A2f, Bf
    emit_ffn1(L, f1, xint, xout)
    return L.finish()

def host_inputs_A(inputs, hf, b):
    x = np.asarray(inputs["L0_attn_x"][b])
    xin = fm_image(x[tok_index(hf)])
    return {"xin": xin}

def run_launch(builders, per_core_inputs):
    L = builders
    in_maps = []
    for c in range(8):
        m = dict(L.inputs)
        m.update(per_core_inputs[c])
        in_maps.append(m)
    res = run_bass_kernel_spmd(L.nc, in_maps, core_ids=list(range(8)))
    return res.results

def _run(L, per_core):
    in_maps = []
    for c in range(8):
        m = dict(L.inputs)
        m.update(per_core[c])
        in_maps.append(m)
    res = run_bass_kernel_spmd(L.nc, in_maps, core_ids=list(range(8)))
    return res.results

def kernel(**inputs):
    inputs = {k: np.asarray(v) for k, v in inputs.items()}
    LA = build_launch_A(inputs, 0)
    LB1 = build_launch_B1(inputs, 0)
    LB2 = build_launch_B2(inputs, 0)
    LC = build_launch_C(inputs, 0)

    per = []
    for c in range(8):
        b, hf = c // 2, c % 2
        per.append({"xin": fm_image(np.asarray(inputs["L0_attn_x"][b])[tok_index(hf)])})
    rA = _run(LA, per)

    per = []
    for c in range(8):
        b, hf = c // 2, c % 2
        k0, k1 = rA[2 * b]["kout"], rA[2 * b + 1]["kout"]
        v0, v1 = rA[2 * b]["vout"], rA[2 * b + 1]["vout"]
        per.append({
            "xres_i": rA[c]["xres_o"],
            "qfl": rA[c]["qout"],
            "kctx": host_assemble_ctx([k0, k1], None),
            "vctx": host_assemble_ctx([v0, v1], None),
            "maskin": host_masks(hf),
        })
    rB1 = _run(LB1, per)

    per = []
    for c in range(8):
        b, hf = c // 2, c % 2
        ti = tok_index(hf)
        per.append({
            "xres_i": rB1[c]["xres_o"],
            "ropecos_own": np.ascontiguousarray(CONSTS["ropecos"][:, ti]),
            "ropesin_own": np.ascontiguousarray(CONSTS["ropesin"][:, ti]),
        })
    rB2 = _run(LB2, per)

    per = []
    for c in range(8):
        b, hf = c // 2, c % 2
        k0, k1 = rB2[2 * b]["k1out"], rB2[2 * b + 1]["k1out"]
        v0, v1 = rB2[2 * b]["v1out"], rB2[2 * b + 1]["v1out"]
        per.append({
            "xres_i": rB2[c]["xres_o"],
            "qfl": rB2[c]["q1out"],
            "kctx": host_assemble_ctx([k0, k1], None),
            "vctx": host_assemble_ctx([v0, v1], None),
            "maskin": host_masks(hf),
            "abrows": rB2[c]["abrows"],
        })
    rC = _run(LC, per)

    out = np.empty((4, 4096, 2048), np.float32)
    for c in range(8):
        b, hf = c // 2, c % 2
        out[b][tok_index(hf)] = rC[c]["xout"].transpose(2, 0, 1).reshape(T, D)
    return out
